# version 18
# baseline (speedup 1.0000x reference)
import numpy as np
import concourse.bass as bass
import concourse.mybir as mybir
from concourse.bass_utils import run_bass_kernel_spmd

F32 = mybir.dt.float32
BF16 = mybir.dt.bfloat16
I32 = mybir.dt.int32
AF = mybir.ActivationFunctionType
OP = mybir.AluOpType
AX = mybir.AxisListType

MAGIC = 12582912.0
MAGIC64 = 6755399441055744.0
NCORES = 8
T = 2048
C = 1024
NH = 16
HD = 64
WIN = 256
NW = T // WIN
HID = 4096
NG = 4
GH = HID // NG
NTC = T // 128
NCC = C // 128
TS = 512
NSL = T // TS


class SemC:
    def __init__(self, es, nc, name):
        self.h = es.enter_context(nc.semaphore(name))
        self.n = 0


def build_nc(debug=False):
    from contextlib import ExitStack
    es = ExitStack()
    nc = bass.Bass("TRN2", target_bir_lowering=False,
                   detect_race_conditions=False, dynamic_dma_scratch_size=8192)

    x_h = nc.dram_tensor("x", [T, C], F32, kind="ExternalInput")
    wqkv_h = nc.dram_tensor("wqkvT", [C, 3 * C], F32, kind="ExternalInput")
    wproj_h = nc.dram_tensor("wprojT", [C, C], F32, kind="ExternalInput")
    w1_h = nc.dram_tensor("w1T", [C, HID], F32, kind="ExternalInput")
    w2_h = nc.dram_tensor("w2T", [HID, C], F32, kind="ExternalInput")
    b1_h = nc.dram_tensor("b1r", [128, 32], F32, kind="ExternalInput")
    b2_h = nc.dram_tensor("b2", [C], F32, kind="ExternalInput")
    identf_h = nc.dram_tensor("identf", [128, 128], F32, kind="ExternalInput")
    onesb_h = nc.dram_tensor("onesb", [128, 128], BF16, kind="ExternalInput")
    cstv_h = nc.dram_tensor("cstv", [128, 8], F32, kind="ExternalInput")
    y_h = nc.dram_tensor("y", [T, C], F32, kind="ExternalOutput")

    acc = nc.alloc_sbuf_tensor("acc", [128, NTC, C], F32)
    xn_f = nc.alloc_sbuf_tensor("xn_f", [128, NCC, T], BF16)
    b1s = nc.alloc_sbuf_tensor("b1s", [128, 32], F32)
    b1q16 = nc.alloc_sbuf_tensor("b1q16", [128, 32], F32)
    ident = nc.alloc_sbuf_tensor("ident", [128, 128], F32)
    ones = nc.alloc_sbuf_tensor("ones", [128, 128], BF16)
    cst = nc.alloc_sbuf_tensor("cst", [128, 8], F32)
    st_s1 = nc.alloc_sbuf_tensor("st_s1", [128, NTC], F32)
    st_s2 = nc.alloc_sbuf_tensor("st_s2", [128, NTC], F32)
    st_mean = nc.alloc_sbuf_tensor("st_mean", [128, NTC], F32)
    st_m2 = nc.alloc_sbuf_tensor("st_m2", [128, NTC], F32)
    st_rstd = nc.alloc_sbuf_tensor("st_rstd", [128, NTC], F32)
    st_nb = nc.alloc_sbuf_tensor("st_nb", [128, NTC], F32)
    st_veps = nc.alloc_sbuf_tensor("st_veps", [128, NTC], F32)
    md_t = nc.alloc_sbuf_tensor("md_t", [128, 1], F32)
    mb_t = nc.alloc_sbuf_tensor("mb_t", [128, 1], F32)
    nmb_t = nc.alloc_sbuf_tensor("nmb_t", [128, 1], F32)

    AR = 120832
    lo, hi = nc.bump_sbuf(AR)
    assert hi - lo >= AR, (lo, hi)

    def at(name, shape, dtype, off):
        nbytes = int(np.prod(shape[1:])) * mybir.dt.size(dtype)
        assert off + nbytes <= AR, (name, off, nbytes, off + nbytes)
        return nc.alloc_sbuf_tensor_at(name, shape, dtype, offset=lo + off)

    w1g = [at(f"w1g{i}", [128, NCC, GH], BF16, 32768 * i) for i in range(2)]
    w2g = [at(f"w2g{i}", [128, 8, C], BF16, 32768 * i + 16384) for i in range(2)]
    stage = at("stage", [128, 2, 512], F32, 65536)
    hq = at("hq", [128, 8, TS], BF16, 69632)
    gpt2 = at("gpt2", [128, 512], F32, 77824)
    t2h = at("t2h", [128, TS], F32, 79872)
    hs1 = at("hs1", [128, TS], F32, 81920)
    b2q = at("b2q", [128, C], F32, 83968)
    stage2 = at("stage2", [128, 2, 512], F32, 83968)
    xn32 = at("xn32", [128, C], F32, 49152)
    t2q = at("t2q", [128, C], F32, 53248)
    xn32b = at("xn32b", [128, C], F32, 57344)
    t2qb = at("t2qb", [128, C], F32, 61440)
    xq_f = at("xq_f", [128, NCC, T], BF16, 88064)
    q_w2 = [at(f"q_w{b}", [128, NCC, WIN], BF16, 88064 + b * 12288) for b in range(2)]
    k_w2 = [at(f"k_w{b}", [128, NCC, WIN], BF16, 92160 + b * 12288) for b in range(2)]
    v_w2 = [at(f"v_w{b}", [128, 2, C], BF16, 96256 + b * 12288) for b in range(2)]
    wq_s = at("wq_s", [128, NCC, C], BF16, 0)
    wk_s = at("wk_s", [128, NCC, C], BF16, 16384)
    wv_s = at("wv_s", [128, NCC, C], BF16, 32768)
    wp_s = at("wp_s", [128, NCC, C], BF16, 49152)
    nl_t = at("nl_t", [128, WIN], F32, 78848)
    y_w = at("y_w", [128, NCC, WIN], BF16, 79872)
    e_t = at("e_t", [128, 4, WIN], BF16, 83968)
    rho_t = at("rho_t", [128, 2, WIN], F32, 86016)

    pb = [nc.alloc_psum_tensor(f"pb{i}", [128, 512], F32) for i in range(8)]

    sq_x = SemC(es, nc, "sq_x")
    sq_w = SemC(es, nc, "sq_w")
    sq_o = SemC(es, nc, "sq_o")
    sg_d = SemC(es, nc, "sg_d")
    s_t = SemC(es, nc, "s_t")
    s_v = SemC(es, nc, "s_v")
    s_s = SemC(es, nc, "s_s")
    s_g = SemC(es, nc, "s_g")

    prog = {k: [] for k in ("T", "V", "S", "G", "Y")}

    def emit(lst, fn):
        prog[lst].append(fn)

    def inc(lst, fn, sem, k=1, dma=False):
        sem.n += k
        if lst in ("V", "S", "G") and not dma:
            prog[lst].append(fn)
            prog[lst].append(lambda eng, s=sem, kk=k: eng.drain().then_inc(s.h, kk))
        else:
            prog[lst].append(lambda eng, f=fn, s=sem, kk=k: f(eng).then_inc(s.h, kk))
        return sem.n

    def wait(lst, sem, val):
        if val > 0:
            prog[lst].append(lambda eng, s=sem, v=val: eng.wait_ge(s.h, v))

    inc_late = inc

    c_ld = inc("Y", lambda y: y.dma_start(ident[:], identf_h.ap()), sq_w, 16)
    c_ld = inc("Y", lambda y: y.dma_start(ones[:], onesb_h.ap()), sq_w, 16)
    c_ld = inc("Y", lambda y: y.dma_start(cst[:], cstv_h.ap()), sq_w, 16)
    wait("T", sq_w, c_ld)
    wait("S", sq_w, c_ld)
    wait("V", sq_w, c_ld)

    emit("V", lambda v: v.tensor_scalar(md_t[:], ones[0:128, 0:1], 0.5, MAGIC, OP.mult, OP.add))
    emit("V", lambda v: v.drain())
    emit("V", lambda v: v.tensor_scalar(md_t[:], md_t[:], MAGIC, 2.0, OP.subtract, OP.mult))
    emit("V", lambda v: v.drain())
    emit("V", lambda v: v.tensor_scalar(mb_t[:], md_t[:], MAGIC64, MAGIC, OP.mult, OP.add))
    emit("V", lambda v: v.drain())
    emit("V", lambda v: v.tensor_scalar(nmb_t[:], mb_t[:], -1.0, None, OP.mult))
    emit("V", lambda v: v.drain())

    b1_ld = inc("Y", lambda y: y.dma_start(b1s[:], b1_h.ap()), sq_w, 16)
    b2_ld = inc("Y", lambda y: y.dma_start(b2q[:], bass.AP(b2_h, 0, [[0, 128], [1, C]])), sq_w, 16)

    x_ld = []
    for tc in range(NTC):
        x_ld.append(inc("Y", lambda y, tc=tc: y.dma_start(acc[:, tc, :],
                                                          x_h.ap()[tc * 128:(tc + 1) * 128, :]), sq_x, 16))

    wait("V", sq_w, b2_ld)
    emit("V", lambda v: v.tensor_scalar(t2q[:], b2q[:], 64.0, -128.49, OP.mult, OP.max))
    emit("V", lambda v: v.drain())
    emit("V", lambda v: v.tensor_scalar(t2q[:], t2q[:], 127.49, mb_t[:, 0:1], OP.min, OP.add))
    emit("V", lambda v: v.drain())
    emit("V", lambda v: v.tensor_scalar(b2q[:], t2q[:], mb_t[:, 0:1], 1.0 / 64.0, OP.subtract, OP.mult))
    emit("V", lambda v: v.drain())
    wait("V", sq_w, b1_ld)
    emit("V", lambda v: v.tensor_scalar(t2q[:, 0:32], b1s[:], 64.0, -128.49, OP.mult, OP.max))
    emit("V", lambda v: v.drain())
    emit("V", lambda v: v.tensor_scalar(t2q[:, 0:32], t2q[:, 0:32], 127.49, mb_t[:, 0:1], OP.min, OP.add))
    emit("V", lambda v: v.drain())
    bq_done = inc("V", lambda v: v.tensor_scalar(b1q16[:], t2q[:, 0:32], mb_t[:, 0:1], 0.25, OP.subtract, OP.mult), s_v)

    w1T_r = w1_h.ap().rearrange("(k p) h -> p k h", p=128)
    wqkv_r = wqkv_h.ap().rearrange("(k p) e -> p k e", p=128)
    wproj_r = wproj_h.ap().rearrange("(k p) o -> p k o", p=128)
    NPC = 32

    def w_piece_ap(g, i):
        if i < 16:
            k, half = divmod(i, 2)
            c0 = g * GH + half * 512
            return w1T_r[:, k, c0:c0 + 512]
        j, half = divmod(i - 16, 2)
        r0 = (g * 8 + j) * 128
        return w2_h.ap()[r0:r0 + 128, half * 512:half * 512 + 512]

    def w_dst(g, i):
        if i < 16:
            k, half = divmod(i, 2)
            return w1g[g % 2][:, k, half * 512:half * 512 + 512]
        j, half = divmod(i - 16, 2)
        return w2g[g % 2][:, j, half * 512:half * 512 + 512]

    def attw_src(wi, k, half):
        if wi < 3:
            return wqkv_r[:, k, wi * C + half * 512: wi * C + half * 512 + 512]
        return wproj_r[:, k, half * 512:half * 512 + 512]

    stage_free = {}
    gp_ctr = [0]
    SLOT_T = {0: (stage, 0), 1: (stage, 1), 2: (stage2, 0), 3: (stage2, 1)}
    w1_q_done = {}
    w2_q_done = {}
    attw_done = {}

    def drain_piece(job, pool):
        kind, src, dst, wait_t, mark = job
        slot = pool[gp_ctr[0] % len(pool)]
        gp_ctr[0] += 1
        st_tile, sub = SLOT_T[slot]
        if slot in stage_free:
            wait("Y", s_s, stage_free[slot])
        ld = inc("Y", lambda y, st_tile=st_tile, sub=sub, src=src:
                 y.dma_start(st_tile[:, sub, :], src), sq_w, 16)
        st = st_tile[:, sub, :]
        if kind == 'q':
            wait("V", sq_w, ld)
            emit("V", lambda v, st=st: v.tensor_scalar(st, st, 64.0, -128.49, OP.mult, OP.max))
            emit("V", lambda v: v.drain())
            vq = inc("V", lambda v, st=st: v.tensor_scalar(st, st, 127.49, mb_t[:, 0:1],
                                                           OP.min, OP.add), s_v)
            wait("S", s_v, vq)
            if wait_t:
                wait("S", s_t, wait_t)
            done = inc("S", lambda s, st=st, dst=dst: s.activation(dst, st, AF.Identity,
                                                                   bias=nmb_t[:, 0:1], scale=1.0), s_s)
        else:
            wait("S", sq_w, ld)
            if wait_t:
                wait("S", s_t, wait_t)
            done = inc("S", lambda s, st=st, dst=dst: s.activation(dst, st, AF.Identity,
                                                                   bias=0.0, scale=1.0), s_s)
        stage_free[slot] = done
        if mark:
            mark(done)
        return done

    def w_jobs(g, lo, hi, wait_t=0):
        jobs = []
        for i in range(lo, hi):
            def mk(g=g, i=i):
                def mark(done):
                    if i == 15:
                        w1_q_done[g] = done
                    if i == 31:
                        w2_q_done[g] = done
                return mark
            jobs.append(('q', w_piece_ap(g, i), w_dst(g, i), wait_t, mk()))
        return jobs

    attw_dst_t_names = None

    def attw_jobs(wi, dst_tile, wait_t=0):
        jobs = []
        for k in range(NCC):
            for half in range(2):
                mark = None
                if k == NCC - 1 and half == 1:
                    def mark(done, wi=wi):
                        attw_done[wi] = done
                jobs.append(('c', attw_src(wi, k, half),
                             dst_tile[:, k, half * 512:half * 512 + 512], wait_t, mark))
        return jobs

    xn_done_s = 0
    accx_done_v = {}

    sums_done = 0
    for tc in range(NTC):
        wait("S", sq_x, x_ld[tc])
        if tc == 0:
            wait("S", s_v, bq_done)
        emit("S", lambda s, tc=tc: s.activation(t2q[:], acc[:, tc, :], AF.Identity,
                                                accum_out=st_s1[:, tc:tc + 1]))
        fn = (lambda s, tc=tc: s.activation(t2q[:], acc[:, tc, :], AF.Square,
                                            accum_out=st_s2[:, tc:tc + 1]))
        if tc == NTC - 1:
            sums_done = inc("S", fn, s_s)
        else:
            emit("S", fn)

    def vdr(fn):
        emit("V", fn)
        emit("V", lambda v: v.drain())

    wait("V", s_s, sums_done)
    vdr(lambda v: v.tensor_scalar(st_mean[:], st_s1[:], 1.0 / C, None, OP.mult))
    vdr(lambda v: v.tensor_tensor(st_m2[:], st_mean[:], st_mean[:], OP.mult))
    vdr(lambda v: v.tensor_scalar(st_veps[:], st_s2[:], 1.0 / C, None, OP.mult))
    vdr(lambda v: v.tensor_tensor(st_veps[:], st_veps[:], st_m2[:], OP.subtract))
    vdr(lambda v: v.tensor_scalar(st_veps[:], st_veps[:], 1.0, 1e-5, OP.mult, OP.add))
    vdr(lambda v: v.tensor_scalar(st_rstd[:], st_veps[:], 0.0, 1.0, OP.mult, OP.add))
    for _nr in range(5):
        vdr(lambda v: v.tensor_tensor(st_m2[:], st_rstd[:], st_rstd[:], OP.mult))
        vdr(lambda v: v.tensor_tensor(st_m2[:], st_m2[:], st_veps[:], OP.mult))
        vdr(lambda v: v.tensor_scalar(st_m2[:], st_m2[:], -0.5, 1.5, OP.mult, OP.add))
        vdr(lambda v: v.tensor_tensor(st_rstd[:], st_rstd[:], st_m2[:], OP.mult))
    vdr(lambda v: v.tensor_scalar(st_nb[:], st_mean[:], -1.0, None, OP.mult))
    stats_done = inc("V", lambda v: v.tensor_tensor(st_nb[:], st_nb[:], st_rstd[:], OP.mult), s_v)
    wait("S", s_v, stats_done)

    mm_bank = [4, 5]
    f2_bank = [2, 3]
    mm_free = {4: 0, 5: 0}
    f2_free = {2: 0, 3: 0}
    ut_free = {0: 0, 1: 0}
    hq_free_t = [0]
    hq_done_m = {}
    f1_last = {}
    f2_last = {}
    grp_compute_done = {}

    def emit_f1_sl(g, sl, f1_jobs):
        wait("T", s_s, w1_q_done[g])
        for tc in range(sl * 4, sl * 4 + 4):
            wait("T", s_s, xq_done_s[tc])
        hqs = []
        for m in range(8):
            mb = mm_bank[m % 2]
            wait("T", s_s, mm_free[mb])
            mm_done = 0
            for k in range(NCC):
                fn = (lambda t, g=g, sl=sl, m=m, k=k, mb=mb:
                      t.matmul(pb[mb][:, :], w1g[g % 2][:, k, m * 128:(m + 1) * 128],
                               xq_f[:, k, sl * TS:(sl + 1) * TS],
                               start=(k == 0), stop=(k == NCC - 1)))
                if k == NCC - 1:
                    mm_done = inc("T", fn, s_t)
                else:
                    emit("T", fn)
            f1_last[g] = mm_done
            ut = t2h if m % 2 == 0 else hs1
            wait("S", s_t, mm_done)
            wait("S", s_v, ut_free[m % 2])
            s1_done = inc("S", lambda s, g=g, m=m, mb=mb, ut=ut: s.activation(
                ut[:], pb[mb][:, :], AF.Relu,
                bias=b1q16[:, g * 8 + m:g * 8 + m + 1], scale=1.0 / 64.0), s_s)
            mm_free[mb] = s1_done
            wait("V", s_s, s1_done)
            if m == 0:
                wait("V", s_t, hq_free_t[0])
            emit("V", lambda v, ut=ut: v.tensor_scalar(ut[:], ut[:], 127.49, mb_t[:, 0:1],
                                                       OP.min, OP.add))
            emit("V", lambda v: v.drain())
            hq_done = inc("V", lambda v, m=m, ut=ut: v.tensor_scalar(hq[:, m, :], ut[:],
                                                                     mb_t[:, 0:1], None, OP.subtract), s_v)
            ut_free[m % 2] = hq_done
            hqs.append(hq_done)
            if f1_jobs:
                drain_piece(f1_jobs.pop(0), (0, 1))
        hq_done_m[(g, sl)] = hqs

    def emit_f2_sl(g, sl, f2_jobs):
        wait("T", s_s, w2_q_done[g])
        hqs = hq_done_m[(g, sl)]
        last_f2 = 0
        for tcc in range(4):
            tc = sl * 4 + tcc
            for os in range(2):
                fb = f2_bank[(tcc * 2 + os) % 2]
                wait("T", s_v, f2_free[fb])
                f2_done = 0
                for j in range(8):
                    if tcc == 0 and os == 0:
                        wait("T", s_v, hqs[j])
                    fn = (lambda t, g=g, j=j, tcc=tcc, os=os, fb=fb:
                          t.matmul(pb[fb][:, :], hq[:, j, tcc * 128:(tcc + 1) * 128],
                                   w2g[g % 2][:, j, os * 512:(os + 1) * 512],
                                   start=(j == 0), stop=(j == 7)))
                    if j == 7:
                        f2_done = inc("T", fn, s_t)
                    else:
                        emit("T", fn)
                last_f2 = f2_done
                wait("V", s_t, f2_done)
                if g == 0:
                    wait("V", s_v, accx_done_v[tc])
                f2_free[fb] = inc("V", lambda v, tc=tc, os=os, fb=fb:
                                  v.scalar_tensor_tensor(acc[:, tc, os * 512:(os + 1) * 512],
                                                         pb[fb][:, :], 1.0 / 1024.0,
                                                         acc[:, tc, os * 512:(os + 1) * 512],
                                                         OP.mult, OP.add), s_v)
                if f2_jobs:
                    drain_piece(f2_jobs.pop(0), (0, 1))
        hq_free_t[0] = last_f2
        f2_last[g] = last_f2
        grp_compute_done[g] = s_v.n

    g0_f2_tail = w_jobs(1, 16, 32)
    ph0_wq = w_jobs(0, 0, 32) + w_jobs(1, 0, 16)
    xn32t = [xn32, xn32b]
    t2qt = [t2q, t2qb]
    xnT_done = {}
    xqT_done = {}
    xn_used_v = {}
    xq_done_s = {}
    last_evac_s = {0: 0, 1: 0, 6: 0, 7: 0}
    for tc in range(NTC):
        par = tc % 2
        xn = xn32t[par]
        tq = t2qt[par]
        wait("S", s_t, xnT_done.get(tc - 2, 0))
        wait("S", s_v, xn_used_v.get(tc - 2, 0))
        xn32_done = inc("S", lambda s, tc=tc, xn=xn: s.activation(xn[:], acc[:, tc, :], AF.Identity,
                                                                  bias=st_nb[:, tc:tc + 1],
                                                                  scale=st_rstd[:, tc:tc + 1]), s_s)
        wait("V", s_s, xn32_done)
        accx_done_v[tc] = inc("V", lambda v, tc=tc: v.tensor_tensor(acc[:, tc, :], acc[:, tc, :],
                                                                    b2q[:], OP.add), s_v)

        wait("T", s_s, xn32_done)
        wait("T", s_s, last_evac_s[1])
        xnT = 0
        for b in (0, 1):
            for j in range(4):
                cc = b * 4 + j
                fn = (lambda t, b=b, j=j, cc=cc, xn=xn:
                      t.transpose(pb[b][:, j * 128:(j + 1) * 128], xn[:, cc * 128:(cc + 1) * 128], ident[:]))
                if b == 1 and j == 3:
                    xnT = inc("T", fn, s_t)
                else:
                    emit("T", fn)
        xnT_done[tc] = xnT
        wait("S", s_t, xnT)
        for b in (0, 1):
            last_evac_s[b] = inc("S", lambda s, b=b, tc=tc: s.activation(
                xn_f[:, b * 4:(b + 1) * 4, tc * 128:(tc + 1) * 128],
                pb[b][:, :].rearrange("p (a b) -> p a b", a=4), AF.Identity, bias=0.0, scale=1.0), s_s)
        xn_done_s = last_evac_s[1]

        wait("V", s_t, xqT_done.get(tc - 2, 0))
        emit("V", lambda v, tq=tq, xn=xn: v.tensor_scalar(tq[:], xn[:], 16.0, -128.49, OP.mult, OP.max))
        xqv_done = inc("V", lambda v, tq=tq: v.tensor_scalar(tq[:], tq[:], 127.49, mb_t[:, 0:1],
                                                             OP.min, OP.add), s_v)
        xn_used_v[tc] = xqv_done

        wait("T", s_v, xqv_done)
        wait("T", s_s, last_evac_s[7])
        xqT = 0
        for b in (6, 7):
            for j in range(4):
                cc = (b - 6) * 4 + j
                fn = (lambda t, b=b, j=j, cc=cc, tq=tq:
                      t.transpose(pb[b][:, j * 128:(j + 1) * 128], tq[:, cc * 128:(cc + 1) * 128], ident[:]))
                if b == 7 and j == 3:
                    xqT = inc("T", fn, s_t)
                else:
                    emit("T", fn)
        xqT_done[tc] = xqT
        wait("S", s_t, xqT)
        for b in (6, 7):
            last_evac_s[b] = inc("S", lambda s, b=b, tc=tc: s.activation(
                xq_f[:, (b - 6) * 4:(b - 5) * 4, tc * 128:(tc + 1) * 128],
                pb[b][:, :].rearrange("p (a b) -> p a b", a=4), AF.Identity,
                bias=nmb_t[:, 0:1], scale=1.0), s_s)
        xq_done_s[tc] = last_evac_s[7]
        for _ in range(4 if tc < 8 else 2):
            if ph0_wq:
                drain_piece(ph0_wq.pop(0), (0, 1))
        if tc % 4 == 3:
            sl = tc // 4
            if sl >= 1:
                emit_f2_sl(0, sl - 1, g0_f2_tail if sl - 1 >= 2 else [])
            emit_f1_sl(0, sl, [])

    while ph0_wq:
        drain_piece(ph0_wq.pop(0), (0, 1))
    emit_f2_sl(0, 3, g0_f2_tail)
    assert not g0_f2_tail
    ph0_v_done = s_v.n



    for g in range(1, NG):
        if g < NG - 1:
            f1j = w_jobs(g + 1, 0, 16, f1_last[g - 1])
            f2j = w_jobs(g + 1, 16, 32, f2_last[g - 1])
        else:
            f1j = attw_jobs(0, wq_s, f1_last[g - 1])
            f2j = attw_jobs(1, wk_s, f2_last[g - 1])
        for sl in range(NSL):
            emit_f1_sl(g, sl, f1j)
            emit_f2_sl(g, sl, f2j)
        assert not f1j and not f2j, (g, len(f1j), len(f2j))

    ffn_all_done_v = s_v.n

    snap_val = 0
    if debug:
        dbg_accffn = nc.dram_tensor("dbg_accffn", [T, C], F32, kind="ExternalOutput")
        wait("Y", s_v, ffn_all_done_v)
        for tc in range(NTC):
            snap_val = inc("Y", lambda y, tc=tc: y.dma_start(dbg_accffn.ap()[tc * 128:(tc + 1) * 128, :],
                                                             acc[:, tc, :]), sq_o, 16)


    wait("T", s_v, ffn_all_done_v)
    wait("T", s_s, xn_done_s)

    s_exp_done = {0: 0, 1: 0}
    l_rho_done = {6: 0, 7: 0}
    att_used_t = {0: 0, 1: 0}
    rho_used_v = {0: 0, 1: 0}
    e_used_t = {0: 0, 1: 0}
    yb_free_v = {0: 0, 1: 0}
    p45_free_v = {4: 0, 5: 0}
    ymm_last = {-1: 0}

    if debug and snap_val:
        wait("V", sq_o, snap_val)

    def emit_qkv_unit(w, u, first_gate):
        buf = w % 2
        b = 4 + (u % 2)
        wait("T", s_v, p45_free_v[b])
        if u < 16:
            wsb, dst = (wq_s, q_w2[buf]) if u < 8 else (wk_s, k_w2[buf])
            m = u % 8
            if w == 0 and m == 0:
                wait("T", s_s, attw_done[0 if u < 8 else 1])
            mm_done = 0
            for k in range(NCC):
                fn = (lambda t, wsb=wsb, m=m, k=k, b=b, w=w:
                      t.matmul(pb[b][:, 0:WIN], wsb[:, k, m * 128:(m + 1) * 128],
                               xn_f[:, k, w * WIN:(w + 1) * WIN],
                               start=(k == 0), stop=(k == NCC - 1)))
                if k == NCC - 1:
                    mm_done = inc("T", fn, s_t)
                else:
                    emit("T", fn)
            wait("V", s_t, max(mm_done, first_gate))
            p45_free_v[b] = inc("V", lambda v, dst=dst, m=m, b=b:
                                v.tensor_copy(dst[:, m, :], pb[b][:, 0:WIN]), s_v)
        else:
            tc2, os = divmod(u - 16, 2)
            if w == 0 and u == 16:
                wait("T", s_s, attw_done[2])
            mm_done = 0
            for k in range(NCC):
                fn = (lambda t, tc2=tc2, os=os, k=k, b=b, w=w:
                      t.matmul(pb[b][:, :], xn_f[:, k, (2 * w + tc2) * 128:(2 * w + tc2 + 1) * 128],
                               wv_s[:, k, os * 512:(os + 1) * 512],
                               start=(k == 0), stop=(k == NCC - 1)))
                if k == NCC - 1:
                    mm_done = inc("T", fn, s_t)
                else:
                    emit("T", fn)
            wait("V", s_t, max(mm_done, first_gate))
            p45_free_v[b] = inc("V", lambda v, buf=buf, tc2=tc2, os=os, b=b:
                                v.tensor_copy(v_w2[buf][:, tc2, os * 512:(os + 1) * 512],
                                              pb[b][:, :]), s_v)

    w0_vw_jobs = attw_jobs(2, wv_s, f1_last[NG - 1])
    w0_wp_jobs = attw_jobs(3, wp_s, f2_last[NG - 1])
    for u in range(20):
        if u < 16 and w0_vw_jobs:
            drain_piece(w0_vw_jobs.pop(0), (0, 1))
        emit_qkv_unit(0, u, 0)
    assert not w0_vw_jobs

    for w in range(NW):
        q_w, k_w, v_w = q_w2[w % 2], k_w2[w % 2], v_w2[w % 2]
        nxt_units = list(range(20)) if w + 1 < NW else []
        for hp in range(NH // 2):
            par = hp % 2
            for hh in range(2):
                h = 2 * hp + hh
                sb = hh
                lb = 6 + hh
                off = hh * 64
                wait("T", s_s, s_exp_done[sb])
                smm = 0
                for jc in range(2):
                    fn = (lambda t, h=h, jc=jc, sb=sb, off=off, k_w=k_w, q_w=q_w:
                          t.matmul(pb[sb][:, jc * WIN:(jc + 1) * WIN],
                                   k_w[off:off + 64, h // 2, jc * 128:(jc + 1) * 128],
                                   q_w[off:off + 64, h // 2, :],
                                   start=True, stop=True, tile_position=(off, 0)))
                    if jc == 1:
                        smm = inc("T", fn, s_t)
                    else:
                        emit("T", fn)
                wait("S", s_t, smm)
                wait("S", s_t, e_used_t[hh])
                exp_done = inc_late("S", lambda s, sb=sb, hh=hh:
                                    s.activation(e_t[:, hh * 2:(hh + 1) * 2, :].rearrange("p a b -> p (a b)"),
                                                 pb[sb][:, :], AF.Exp, bias=cst[:, 0:1], scale=0.125), s_s)
                s_exp_done[sb] = exp_done
                wait("T", s_s, exp_done)
                wait("T", s_s, l_rho_done[lb])
                lmm = 0
                for jc in range(2):
                    fn = (lambda t, jc=jc, lb=lb, hh=hh:
                          t.matmul(pb[lb][:, 0:WIN], ones[:, 0:128], e_t[:, hh * 2 + jc, :],
                                   start=(jc == 0), stop=(jc == 1)))
                    if jc == 1:
                        lmm = inc("T", fn, s_t)
                    else:
                        emit("T", fn)
                e_used_t[hh] = max(e_used_t[hh], lmm)
                wait("S", s_t, lmm)
                emit("S", lambda s, lb=lb: s.activation(nl_t[:], pb[lb][:, 0:WIN], AF.Ln, bias=0.0, scale=1.0))
                wait("S", s_v, rho_used_v[par])
                rho_done = inc_late("S", lambda s, par=par: s.activation(rho_t[:, par, :], nl_t[:], AF.Exp,
                                                                          bias=0.0, scale=-1.0), s_s)
                l_rho_done[lb] = rho_done
                yb = 2 + hh
                wait("T", s_s, exp_done)
                wait("T", s_v, yb_free_v[hh])
                ymm = 0
                for jc in range(2):
                    fn = (lambda t, jc=jc, h=h, yb=yb, off=off, hh=hh, v_w=v_w:
                          t.matmul(pb[yb][off:off + 64, 0:WIN], v_w[:, jc, h * 64:(h + 1) * 64],
                                   e_t[:, hh * 2 + jc, :],
                                   start=(jc == 0), stop=(jc == 1), tile_position=(0, off)))
                    if jc == 1:
                        ymm = inc("T", fn, s_t)
                    else:
                        emit("T", fn)
                e_used_t[hh] = max(e_used_t[hh], ymm)
                att_used_t[par] = ymm
                wait("V", s_t, ymm)
                wait("V", s_s, rho_done)
                done = inc("V", lambda v, hp=hp, yb=yb, off=off, par=par:
                           v.tensor_tensor(y_w[off:off + 64, hp, :], pb[yb][off:off + 64, 0:WIN],
                                           rho_t[off:off + 64, par, :], OP.mult), s_v)
                yb_free_v[hh] = done
                rho_used_v[par] = done
                if nxt_units:
                    emit_qkv_unit(w + 1, nxt_units.pop(0), ymm_last[w - 1])
                if w == 0 and w0_wp_jobs:
                    drain_piece(w0_wp_jobs.pop(0), (0, 1))
            if nxt_units and hp % 2 == 1:
                emit_qkv_unit(w + 1, nxt_units.pop(0), ymm_last[w - 1])
        while nxt_units:
            emit_qkv_unit(w + 1, nxt_units.pop(0), ymm_last[w - 1])
        y_all_evac = s_v.n
        ymm_last[w] = max(att_used_t[0], att_used_t[1])

        if w == 0:
            wait("T", s_s, attw_done[3])
        wait("T", s_v, y_all_evac)
        for tc2 in range(2):
            tc = 2 * w + tc2
            for os in range(2):
                b = 4 + ((tc2 * 2 + os) % 2)
                wait("T", s_v, p45_free_v[b])
                pmm = 0
                for k in range(NCC):
                    fn = (lambda t, k=k, tc2=tc2, os=os, b=b:
                          t.matmul(pb[b][:, :], y_w[:, k, tc2 * 128:(tc2 + 1) * 128],
                                   wp_s[:, k, os * 512:(os + 1) * 512],
                                   start=(k == 0), stop=(k == NCC - 1)))
                    if k == NCC - 1:
                        pmm = inc("T", fn, s_t)
                    else:
                        emit("T", fn)
                wait("V", s_t, pmm)
                p45_free_v[b] = inc("V", lambda v, tc=tc, os=os, b=b:
                                    v.scalar_tensor_tensor(acc[:, tc, os * 512:(os + 1) * 512],
                                                           pb[b][:, :], 1.0,
                                                           acc[:, tc, os * 512:(os + 1) * 512],
                                                           OP.mult, OP.add), s_v)
        wait("Y", s_v, s_v.n)
        for tc2 in range(2):
            tc = 2 * w + tc2
            inc("Y", lambda y, tc=tc: y.dma_start(y_h.ap()[tc * 128:(tc + 1) * 128, :], acc[:, tc, :]),
                sq_o, 16)

    wait("Y", sq_o, sq_o.n)

    if debug:
        dbg = {}
        for nm, tile in (("mean", st_mean), ("rstd", st_rstd), ("veps", st_veps),
                         ("xn_f", xn_f), ("xq_f", xq_f), ("q_w", q_w), ("k_w", k_w),
                         ("v_w", v_w), ("y_w", y_w), ("e_t", e_t), ("rho_t", rho_t),
                         ("b1q16", b1q16)):
            sh = [tile.shape[0], int(np.prod(tile.shape[1:]))]
            d_h = nc.dram_tensor("dbg_" + nm, sh, tile.dtype, kind="ExternalOutput")
            dbg[nm] = (d_h, tile)
        wait("G", s_v, s_v.n)
        wait("G", s_s, s_s.n)
        wait("G", s_t, s_t.n)
        dsem2 = SemC(es, nc, "dsem2")
        for nm, (d_h, tile) in dbg.items():
            flat = tile[:] if len(tile.shape) == 2 else tile[:].rearrange(
                {3: "p a b -> p (a b)", 4: "p a b c -> p (a b c)"}[len(tile.shape)])
            inc("G", lambda g, d_h=d_h, flat=flat: g.dma_start(d_h.ap(), flat), dsem2.h and dsem2, 16, dma=True)
        wait("G", dsem2, dsem2.n)

    with nc.Block() as block:
        @block.tensor
        def _(eng):
            for f in prog["T"]:
                f(eng)

        @block.vector
        def _(eng):
            for f in prog["V"]:
                f(eng)

        @block.scalar
        def _(eng):
            for f in prog["S"]:
                f(eng)

        @block.gpsimd
        def _(eng):
            for f in prog["G"]:
                f(eng)

        @block.sync
        def _(eng):
            for f in prog["Y"]:
                f(eng)

    es.close()
    return nc


_NC_CACHE = None


def kernel(L0_attn_x, L0_attn_w_attn, L0_attn_w_proj, L0_ffn_w1, L0_ffn_b1, L0_ffn_w2, L0_ffn_b2):
    global _NC_CACHE
    x = np.ascontiguousarray(np.asarray(L0_attn_x, dtype=np.float32).reshape(16384, 1024))
    wqkvT = np.ascontiguousarray(np.asarray(L0_attn_w_attn, dtype=np.float32).T)
    wprojT = np.ascontiguousarray(np.asarray(L0_attn_w_proj, dtype=np.float32).T)
    w1T = np.ascontiguousarray(np.asarray(L0_ffn_w1, dtype=np.float32).T)
    w2T = np.ascontiguousarray(np.asarray(L0_ffn_w2, dtype=np.float32).T)
    b1r = np.ascontiguousarray(np.asarray(L0_ffn_b1, dtype=np.float32).reshape(32, 128).T)
    b2 = np.ascontiguousarray(np.asarray(L0_ffn_b2, dtype=np.float32))

    if _NC_CACHE is None:
        _NC_CACHE = build_nc()
    nc = _NC_CACHE

    import ml_dtypes
    identf = np.eye(128, dtype=np.float32)
    onesb = np.ones((128, 128), dtype=ml_dtypes.bfloat16)
    cstv = np.zeros((128, 8), dtype=np.float32)
    cstv[:, 0] = -20.0
    cstv[:, 1] = 1e-5
    in_maps = [{
        "x": x[c * T:(c + 1) * T, :],
        "wqkvT": wqkvT, "wprojT": wprojT,
        "w1T": w1T, "w2T": w2T, "b1r": b1r, "b2": b2,
        "identf": identf, "onesb": onesb, "cstv": cstv,
    } for c in range(NCORES)]
    res = run_bass_kernel_spmd(nc, in_maps, core_ids=list(range(NCORES)))
    out = np.concatenate([np.asarray(r["y"]) for r in res.results], axis=0)
    return out.reshape(1, 16384, 1024)


# version 27
# speedup vs baseline: 1.0632x; 1.0632x over previous
import numpy as np
import concourse.bass as bass
import concourse.mybir as mybir
from concourse.bass_utils import run_bass_kernel_spmd

F32 = mybir.dt.float32
BF16 = mybir.dt.bfloat16
I32 = mybir.dt.int32
AF = mybir.ActivationFunctionType
OP = mybir.AluOpType
AX = mybir.AxisListType

MAGIC = 12582912.0
MAGIC64 = 6755399441055744.0
NCORES = 8
T = 2048
C = 1024
NH = 16
HD = 64
WIN = 256
NW = T // WIN
HID = 4096
NG = 4
GH = HID // NG
NTC = T // 128
NCC = C // 128
TS = 512
NSL = T // TS


class SemC:
    def __init__(self, es, nc, name):
        self.h = es.enter_context(nc.semaphore(name))
        self.n = 0


def build_nc(debug=False):
    from contextlib import ExitStack
    es = ExitStack()
    nc = bass.Bass("TRN2", target_bir_lowering=False,
                   detect_race_conditions=False, dynamic_dma_scratch_size=8192)

    x_h = nc.dram_tensor("x", [T, C], F32, kind="ExternalInput")
    wqkv_h = nc.dram_tensor("wqkvT", [C, 3 * C], F32, kind="ExternalInput")
    wproj_h = nc.dram_tensor("wprojT", [C, C], F32, kind="ExternalInput")
    w1_h = nc.dram_tensor("w1T", [C, HID], F32, kind="ExternalInput")
    w2_h = nc.dram_tensor("w2T", [HID, C], F32, kind="ExternalInput")
    b1_h = nc.dram_tensor("b1r", [128, 32], F32, kind="ExternalInput")
    b2_h = nc.dram_tensor("b2", [C], F32, kind="ExternalInput")
    identf_h = nc.dram_tensor("identf", [128, 128], F32, kind="ExternalInput")
    onesb_h = nc.dram_tensor("onesb", [128, 128], BF16, kind="ExternalInput")
    cstv_h = nc.dram_tensor("cstv", [128, 8], F32, kind="ExternalInput")
    y_h = nc.dram_tensor("y", [T, C], F32, kind="ExternalOutput")

    acc = nc.alloc_sbuf_tensor("acc", [128, NTC, C], F32)
    xn_f = nc.alloc_sbuf_tensor("xn_f", [128, NCC, T], BF16)
    b1s = nc.alloc_sbuf_tensor("b1s", [128, 32], F32)
    b1q16 = nc.alloc_sbuf_tensor("b1q16", [128, 32], F32)
    ident = nc.alloc_sbuf_tensor("ident", [128, 128], F32)
    ones = nc.alloc_sbuf_tensor("ones", [128, 128], BF16)
    cst = nc.alloc_sbuf_tensor("cst", [128, 8], F32)
    st_s1 = nc.alloc_sbuf_tensor("st_s1", [128, NTC], F32)
    st_s2 = nc.alloc_sbuf_tensor("st_s2", [128, NTC], F32)
    st_mean = nc.alloc_sbuf_tensor("st_mean", [128, NTC], F32)
    st_m2 = nc.alloc_sbuf_tensor("st_m2", [128, NTC], F32)
    st_rstd = nc.alloc_sbuf_tensor("st_rstd", [128, NTC], F32)
    st_nb = nc.alloc_sbuf_tensor("st_nb", [128, NTC], F32)
    st_veps = nc.alloc_sbuf_tensor("st_veps", [128, NTC], F32)
    md_t = nc.alloc_sbuf_tensor("md_t", [128, 1], F32)
    mb_t = nc.alloc_sbuf_tensor("mb_t", [128, 1], F32)
    nmb_t = nc.alloc_sbuf_tensor("nmb_t", [128, 1], F32)

    AR = 120832
    lo, hi = nc.bump_sbuf(AR)
    assert hi - lo >= AR, (lo, hi)

    def at(name, shape, dtype, off):
        nbytes = int(np.prod(shape[1:])) * mybir.dt.size(dtype)
        assert off + nbytes <= AR, (name, off, nbytes, off + nbytes)
        return nc.alloc_sbuf_tensor_at(name, shape, dtype, offset=lo + off)

    w1g = [at(f"w1g{i}", [128, NCC, GH], BF16, 32768 * i) for i in range(2)]
    w2g = [at(f"w2g{i}", [128, 8, C], BF16, 32768 * i + 16384) for i in range(2)]
    stage = at("stage", [128, 2, 512], F32, 65536)
    hq = at("hq", [128, 8, TS], BF16, 69632)
    gpt2 = at("gpt2", [128, 512], F32, 77824)
    t2h = at("t2h", [128, TS], F32, 79872)
    hs1 = at("hs1", [128, TS], F32, 81920)
    b2q = at("b2q", [128, C], F32, 83968)
    stage2 = at("stage2", [128, 2, 512], F32, 83968)
    xn32 = at("xn32", [128, C], F32, 49152)
    t2q = at("t2q", [128, C], F32, 53248)
    xn32b = at("xn32b", [128, C], F32, 57344)
    t2qb = at("t2qb", [128, C], F32, 61440)
    xq_f = at("xq_f", [128, NCC, T], BF16, 88064)
    q_w2 = [at(f"q_w{b}", [128, NCC, WIN], BF16, 88064 + b * 12288) for b in range(2)]
    k_w2 = [at(f"k_w{b}", [128, NCC, WIN], BF16, 92160 + b * 12288) for b in range(2)]
    v_w2 = [at(f"v_w{b}", [128, 2, C], BF16, 96256 + b * 12288) for b in range(2)]
    wq_s = at("wq_s", [128, NCC, C], BF16, 0)
    wk_s = at("wk_s", [128, NCC, C], BF16, 16384)
    wv_s = at("wv_s", [128, NCC, C], BF16, 32768)
    wp_s = at("wp_s", [128, NCC, C], BF16, 49152)
    nl_t = at("nl_t", [128, WIN], F32, 78848)
    y_w = at("y_w", [128, NCC, WIN], BF16, 79872)
    e_t = at("e_t", [128, 4, WIN], BF16, 83968)
    rho_t = at("rho_t", [128, 4, WIN], F32, 69632)

    pb = [nc.alloc_psum_tensor(f"pb{i}", [128, 512], F32) for i in range(8)]

    sq_x = SemC(es, nc, "sq_x")
    sq_w = SemC(es, nc, "sq_w")
    sq_o = SemC(es, nc, "sq_o")
    sg_d = SemC(es, nc, "sg_d")
    s_t = SemC(es, nc, "s_t")
    s_v = SemC(es, nc, "s_v")
    s_s = SemC(es, nc, "s_s")
    s_g = SemC(es, nc, "s_g")

    prog = {k: [] for k in ("T", "V", "S", "G", "Y")}

    def emit(lst, fn):
        prog[lst].append(fn)

    def inc(lst, fn, sem, k=1, dma=False):
        sem.n += k
        if lst in ("V", "S", "G") and not dma:
            prog[lst].append(fn)
            prog[lst].append(lambda eng, s=sem, kk=k: eng.drain().then_inc(s.h, kk))
        else:
            prog[lst].append(lambda eng, f=fn, s=sem, kk=k: f(eng).then_inc(s.h, kk))
        return sem.n

    def wait(lst, sem, val):
        if val > 0:
            prog[lst].append(lambda eng, s=sem, v=val: eng.wait_ge(s.h, v))

    inc_late = inc

    c_ld = inc("Y", lambda y: y.dma_start(ident[:], identf_h.ap()), sq_w, 16)
    c_ld = inc("Y", lambda y: y.dma_start(ones[:], onesb_h.ap()), sq_w, 16)
    c_ld = inc("Y", lambda y: y.dma_start(cst[:], cstv_h.ap()), sq_w, 16)
    wait("T", sq_w, c_ld)
    wait("S", sq_w, c_ld)
    wait("V", sq_w, c_ld)

    emit("V", lambda v: v.tensor_scalar(md_t[:], ones[0:128, 0:1], 0.5, MAGIC, OP.mult, OP.add))
    emit("V", lambda v: v.drain())
    emit("V", lambda v: v.tensor_scalar(md_t[:], md_t[:], MAGIC, 2.0, OP.subtract, OP.mult))
    emit("V", lambda v: v.drain())
    emit("V", lambda v: v.tensor_scalar(mb_t[:], md_t[:], MAGIC64, MAGIC, OP.mult, OP.add))
    emit("V", lambda v: v.drain())
    emit("V", lambda v: v.tensor_scalar(nmb_t[:], mb_t[:], -1.0, None, OP.mult))
    emit("V", lambda v: v.drain())

    b1_ld = inc("Y", lambda y: y.dma_start(b1s[:], b1_h.ap()), sq_w, 16)
    b2_ld = inc("Y", lambda y: y.dma_start(b2q[:], bass.AP(b2_h, 0, [[0, 128], [1, C]])), sq_w, 16)

    x_ld = []
    for tc in range(NTC):
        x_ld.append(inc("Y", lambda y, tc=tc: y.dma_start(acc[:, tc, :],
                                                          x_h.ap()[tc * 128:(tc + 1) * 128, :]), sq_x, 16))

    wait("V", sq_w, b2_ld)
    emit("V", lambda v: v.tensor_scalar(t2q[:], b2q[:], 64.0, -128.49, OP.mult, OP.max))
    emit("V", lambda v: v.drain())
    emit("V", lambda v: v.tensor_scalar(t2q[:], t2q[:], 127.49, mb_t[:, 0:1], OP.min, OP.add))
    emit("V", lambda v: v.drain())
    emit("V", lambda v: v.tensor_scalar(b2q[:], t2q[:], mb_t[:, 0:1], 1.0 / 64.0, OP.subtract, OP.mult))
    emit("V", lambda v: v.drain())
    wait("V", sq_w, b1_ld)
    emit("V", lambda v: v.tensor_scalar(t2q[:, 0:32], b1s[:], 64.0, -128.49, OP.mult, OP.max))
    emit("V", lambda v: v.drain())
    emit("V", lambda v: v.tensor_scalar(t2q[:, 0:32], t2q[:, 0:32], 127.49, mb_t[:, 0:1], OP.min, OP.add))
    emit("V", lambda v: v.drain())
    bq_done = inc("V", lambda v: v.tensor_scalar(b1q16[:], t2q[:, 0:32], mb_t[:, 0:1], 0.25, OP.subtract, OP.mult), s_v)

    w1T_r = w1_h.ap().rearrange("(k p) h -> p k h", p=128)
    wqkv_r = wqkv_h.ap().rearrange("(k p) e -> p k e", p=128)
    wproj_r = wproj_h.ap().rearrange("(k p) o -> p k o", p=128)
    NPC = 32

    def w_piece_ap(g, i):
        if i < 16:
            k, half = divmod(i, 2)
            c0 = g * GH + half * 512
            return w1T_r[:, k, c0:c0 + 512]
        j, half = divmod(i - 16, 2)
        r0 = (g * 8 + j) * 128
        return w2_h.ap()[r0:r0 + 128, half * 512:half * 512 + 512]

    def w_dst(g, i):
        if i < 16:
            k, half = divmod(i, 2)
            return w1g[g % 2][:, k, half * 512:half * 512 + 512]
        j, half = divmod(i - 16, 2)
        return w2g[g % 2][:, j, half * 512:half * 512 + 512]

    def attw_src(wi, k, half):
        if wi < 3:
            return wqkv_r[:, k, wi * C + half * 512: wi * C + half * 512 + 512]
        return wproj_r[:, k, half * 512:half * 512 + 512]

    stage_free = {}
    gp_ctr = [0]
    SLOT_T = {0: (stage, 0), 1: (stage, 1), 2: (stage2, 0), 3: (stage2, 1)}
    w1_q_done = {}
    w2_q_done = {}
    attw_done = {}

    def drain_piece(job, pool):
        kind, src, dst, wait_t, mark = job
        slot = pool[gp_ctr[0] % len(pool)]
        gp_ctr[0] += 1
        st_tile, sub = SLOT_T[slot]
        if slot in stage_free:
            wait("Y", s_v, stage_free[slot])
        ld = inc("Y", lambda y, st_tile=st_tile, sub=sub, src=src:
                 y.dma_start(st_tile[:, sub, :], src), sq_w, 16)
        st = st_tile[:, sub, :]
        wait("V", sq_w, ld)
        if wait_t:
            wait("V", s_t, wait_t)
        if kind == 'q':
            emit("V", lambda v, st=st: v.tensor_scalar(st, st, 64.0, -128.49, OP.mult, OP.max))
            emit("V", lambda v: v.drain())
            emit("V", lambda v, st=st: v.tensor_scalar(st, st, 127.49, mb_t[:, 0:1],
                                                       OP.min, OP.add))
            emit("V", lambda v: v.drain())
            done = inc("V", lambda v, st=st, dst=dst: v.tensor_scalar(dst, st, mb_t[:, 0:1],
                                                                      None, OP.subtract), s_v)
        else:
            done = inc("V", lambda v, st=st, dst=dst: v.tensor_copy(dst, st), s_v)
        stage_free[slot] = done
        if mark:
            mark(done)
        return done

    def w_jobs(g, lo, hi, wait_t=0):
        jobs = []
        for i in range(lo, hi):
            def mk(g=g, i=i):
                def mark(done):
                    if i == 15:
                        w1_q_done[g] = done
                    if i == 31:
                        w2_q_done[g] = done
                return mark
            jobs.append(('q', w_piece_ap(g, i), w_dst(g, i), wait_t, mk()))
        return jobs

    attw_dst_t_names = None

    def attw_jobs(wi, dst_tile, wait_t=0):
        jobs = []
        for k in range(NCC):
            for half in range(2):
                mark = None
                if k == NCC - 1 and half == 1:
                    def mark(done, wi=wi):
                        attw_done[wi] = done
                jobs.append(('c', attw_src(wi, k, half),
                             dst_tile[:, k, half * 512:half * 512 + 512], wait_t, mark))
        return jobs

    xn_done_s = 0
    accx_done_v = {}

    sums_done = 0
    for tc in range(NTC):
        wait("S", sq_x, x_ld[tc])
        if tc == 0:
            wait("S", s_v, bq_done)
        emit("S", lambda s, tc=tc: s.activation(t2q[:], acc[:, tc, :], AF.Identity,
                                                accum_out=st_s1[:, tc:tc + 1]))
        fn = (lambda s, tc=tc: s.activation(t2q[:], acc[:, tc, :], AF.Square,
                                            accum_out=st_s2[:, tc:tc + 1]))
        if tc == NTC - 1:
            sums_done = inc("S", fn, s_s)
        else:
            emit("S", fn)

    def vdr(fn):
        emit("V", fn)
        emit("V", lambda v: v.drain())

    wait("V", s_s, sums_done)
    vdr(lambda v: v.tensor_scalar(st_mean[:], st_s1[:], 1.0 / C, None, OP.mult))
    vdr(lambda v: v.tensor_tensor(st_m2[:], st_mean[:], st_mean[:], OP.mult))
    vdr(lambda v: v.tensor_scalar(st_veps[:], st_s2[:], 1.0 / C, None, OP.mult))
    vdr(lambda v: v.tensor_tensor(st_veps[:], st_veps[:], st_m2[:], OP.subtract))
    vdr(lambda v: v.tensor_scalar(st_veps[:], st_veps[:], 1.0, 1e-5, OP.mult, OP.add))
    vdr(lambda v: v.tensor_scalar(st_rstd[:], st_veps[:], 0.0, 1.0, OP.mult, OP.add))
    for _nr in range(5):
        vdr(lambda v: v.tensor_tensor(st_m2[:], st_rstd[:], st_rstd[:], OP.mult))
        vdr(lambda v: v.tensor_tensor(st_m2[:], st_m2[:], st_veps[:], OP.mult))
        vdr(lambda v: v.tensor_scalar(st_m2[:], st_m2[:], -0.5, 1.5, OP.mult, OP.add))
        vdr(lambda v: v.tensor_tensor(st_rstd[:], st_rstd[:], st_m2[:], OP.mult))
    vdr(lambda v: v.tensor_scalar(st_nb[:], st_mean[:], -1.0, None, OP.mult))
    stats_done = inc("V", lambda v: v.tensor_tensor(st_nb[:], st_nb[:], st_rstd[:], OP.mult), s_v)
    wait("S", s_v, stats_done)

    mm_bank = [4, 5]
    f2_bank = [2, 3]
    mm_free = {4: 0, 5: 0}
    f2_free = {2: 0, 3: 0}
    ut_free = {0: 0, 1: 0}
    hq_free_t = [0]
    hq_done_m = {}
    f1_last = {}
    f2_last = {}
    grp_compute_done = {}

    def emit_f1_sl(g, sl, f1_jobs):
        wait("T", s_v, w1_q_done[g])
        for tc in range(sl * 4, sl * 4 + 4):
            wait("T", s_s, xq_done_s[tc])
        hqs = []
        for m in range(8):
            mb = mm_bank[m % 2]
            wait("T", s_s, mm_free[mb])
            mm_done = 0
            for k in range(NCC):
                fn = (lambda t, g=g, sl=sl, m=m, k=k, mb=mb:
                      t.matmul(pb[mb][:, :], w1g[g % 2][:, k, m * 128:(m + 1) * 128],
                               xq_f[:, k, sl * TS:(sl + 1) * TS],
                               start=(k == 0), stop=(k == NCC - 1)))
                if k == NCC - 1:
                    mm_done = inc("T", fn, s_t)
                else:
                    emit("T", fn)
            f1_last[g] = mm_done
            ut = t2h if m % 2 == 0 else hs1
            wait("S", s_t, mm_done)
            wait("S", s_v, ut_free[m % 2])
            s1_done = inc("S", lambda s, g=g, m=m, mb=mb, ut=ut: s.activation(
                ut[:], pb[mb][:, :], AF.Relu,
                bias=b1q16[:, g * 8 + m:g * 8 + m + 1], scale=1.0 / 64.0), s_s)
            mm_free[mb] = s1_done
            wait("V", s_s, s1_done)
            if m == 0:
                wait("V", s_t, hq_free_t[0])
            emit("V", lambda v, ut=ut: v.tensor_scalar(ut[:], ut[:], 127.49, mb_t[:, 0:1],
                                                       OP.min, OP.add))
            emit("V", lambda v: v.drain())
            hq_done = inc("V", lambda v, m=m, ut=ut: v.tensor_scalar(hq[:, m, :], ut[:],
                                                                     mb_t[:, 0:1], None, OP.subtract), s_v)
            ut_free[m % 2] = hq_done
            hqs.append(hq_done)
            if f1_jobs:
                drain_piece(f1_jobs.pop(0), (0, 1))
        hq_done_m[(g, sl)] = hqs

    def emit_f2_sl(g, sl, f2_jobs):
        wait("T", s_v, w2_q_done[g])
        hqs = hq_done_m[(g, sl)]
        last_f2 = 0
        for tcc in range(4):
            tc = sl * 4 + tcc
            for os in range(2):
                fb = f2_bank[(tcc * 2 + os) % 2]
                wait("T", s_v, f2_free[fb])
                f2_done = 0
                for j in range(8):
                    if tcc == 0 and os == 0:
                        wait("T", s_v, hqs[j])
                    fn = (lambda t, g=g, j=j, tcc=tcc, os=os, fb=fb:
                          t.matmul(pb[fb][:, :], hq[:, j, tcc * 128:(tcc + 1) * 128],
                                   w2g[g % 2][:, j, os * 512:(os + 1) * 512],
                                   start=(j == 0), stop=(j == 7)))
                    if j == 7:
                        f2_done = inc("T", fn, s_t)
                    else:
                        emit("T", fn)
                last_f2 = f2_done
                wait("V", s_t, f2_done)
                if g == 0:
                    wait("V", s_v, accx_done_v[tc])
                f2_free[fb] = inc("V", lambda v, tc=tc, os=os, fb=fb:
                                  v.scalar_tensor_tensor(acc[:, tc, os * 512:(os + 1) * 512],
                                                         pb[fb][:, :], 1.0 / 1024.0,
                                                         acc[:, tc, os * 512:(os + 1) * 512],
                                                         OP.mult, OP.add), s_v)
                if f2_jobs:
                    drain_piece(f2_jobs.pop(0), (0, 1))
        hq_free_t[0] = last_f2
        f2_last[g] = last_f2
        grp_compute_done[g] = s_v.n

    g0_f2_tail = w_jobs(1, 16, 32)
    ph0_wq = w_jobs(0, 0, 32) + w_jobs(1, 0, 16)
    xn32t = [xn32, xn32b]
    t2qt = [t2q, t2qb]
    xnT_done = {}
    xqT_done = {}
    xn_used_v = {}
    xq_done_s = {}
    last_evac_s = {0: 0, 1: 0, 6: 0, 7: 0}
    for tc in range(NTC):
        par = tc % 2
        xn = xn32t[par]
        tq = t2qt[par]
        wait("S", s_t, xnT_done.get(tc - 2, 0))
        wait("S", s_v, xn_used_v.get(tc - 2, 0))
        xn32_done = inc("S", lambda s, tc=tc, xn=xn: s.activation(xn[:], acc[:, tc, :], AF.Identity,
                                                                  bias=st_nb[:, tc:tc + 1],
                                                                  scale=st_rstd[:, tc:tc + 1]), s_s)
        wait("V", s_s, xn32_done)
        accx_done_v[tc] = inc("V", lambda v, tc=tc: v.tensor_tensor(acc[:, tc, :], acc[:, tc, :],
                                                                    b2q[:], OP.add), s_v)

        wait("T", s_s, xn32_done)
        wait("T", s_s, last_evac_s[1])
        xnT = 0
        for b in (0, 1):
            for j in range(4):
                cc = b * 4 + j
                fn = (lambda t, b=b, j=j, cc=cc, xn=xn:
                      t.transpose(pb[b][:, j * 128:(j + 1) * 128], xn[:, cc * 128:(cc + 1) * 128], ident[:]))
                if b == 1 and j == 3:
                    xnT = inc("T", fn, s_t)
                else:
                    emit("T", fn)
        xnT_done[tc] = xnT
        wait("S", s_t, xnT)
        for b in (0, 1):
            last_evac_s[b] = inc("S", lambda s, b=b, tc=tc: s.activation(
                xn_f[:, b * 4:(b + 1) * 4, tc * 128:(tc + 1) * 128],
                pb[b][:, :].rearrange("p (a b) -> p a b", a=4), AF.Identity, bias=0.0, scale=1.0), s_s)
        xn_done_s = last_evac_s[1]

        wait("V", s_t, xqT_done.get(tc - 2, 0))
        emit("V", lambda v, tq=tq, xn=xn: v.tensor_scalar(tq[:], xn[:], 16.0, -128.49, OP.mult, OP.max))
        xqv_done = inc("V", lambda v, tq=tq: v.tensor_scalar(tq[:], tq[:], 127.49, mb_t[:, 0:1],
                                                             OP.min, OP.add), s_v)
        xn_used_v[tc] = xqv_done

        wait("T", s_v, xqv_done)
        wait("T", s_s, last_evac_s[7])
        xqT = 0
        for b in (6, 7):
            for j in range(4):
                cc = (b - 6) * 4 + j
                fn = (lambda t, b=b, j=j, cc=cc, tq=tq:
                      t.transpose(pb[b][:, j * 128:(j + 1) * 128], tq[:, cc * 128:(cc + 1) * 128], ident[:]))
                if b == 7 and j == 3:
                    xqT = inc("T", fn, s_t)
                else:
                    emit("T", fn)
        xqT_done[tc] = xqT
        wait("S", s_t, xqT)
        for b in (6, 7):
            last_evac_s[b] = inc("S", lambda s, b=b, tc=tc: s.activation(
                xq_f[:, (b - 6) * 4:(b - 5) * 4, tc * 128:(tc + 1) * 128],
                pb[b][:, :].rearrange("p (a b) -> p a b", a=4), AF.Identity,
                bias=nmb_t[:, 0:1], scale=1.0), s_s)
        xq_done_s[tc] = last_evac_s[7]
        for _ in range(4 if tc < 8 else 2):
            if ph0_wq:
                drain_piece(ph0_wq.pop(0), (0, 1))
        if tc % 4 == 3:
            sl = tc // 4
            if sl >= 1:
                emit_f2_sl(0, sl - 1, g0_f2_tail if sl - 1 >= 2 else [])
            emit_f1_sl(0, sl, [])

    while ph0_wq:
        drain_piece(ph0_wq.pop(0), (0, 1))
    emit_f2_sl(0, 3, g0_f2_tail)
    assert not g0_f2_tail
    ph0_v_done = s_v.n



    for g in range(1, NG):
        if g < NG - 1:
            f1j = w_jobs(g + 1, 0, 16, f1_last[g - 1])
            f2j = w_jobs(g + 1, 16, 32, f2_last[g - 1])
        else:
            f1j = attw_jobs(0, wq_s, f1_last[g - 1])
            f2j = attw_jobs(1, wk_s, f2_last[g - 1])
        for sl in range(NSL):
            emit_f1_sl(g, sl, f1j)
            emit_f2_sl(g, sl, f2j)
        assert not f1j and not f2j, (g, len(f1j), len(f2j))

    ffn_all_done_v = s_v.n

    snap_val = 0
    if debug:
        dbg_accffn = nc.dram_tensor("dbg_accffn", [T, C], F32, kind="ExternalOutput")
        wait("Y", s_v, ffn_all_done_v)
        for tc in range(NTC):
            snap_val = inc("Y", lambda y, tc=tc: y.dma_start(dbg_accffn.ap()[tc * 128:(tc + 1) * 128, :],
                                                             acc[:, tc, :]), sq_o, 16)


    wait("T", s_v, ffn_all_done_v)
    wait("T", s_s, xn_done_s)

    s_exp_done = {0: 0, 1: 0}
    l_rho_done = {6: 0, 7: 0}
    att_used_t = {0: 0, 1: 0}
    rho_used_v = {}
    e_used_t = {0: 0, 1: 0}
    yb_free_v = {0: 0, 1: 0}
    p45_free_v = {4: 0, 5: 0}
    ymm_last = {-1: 0}

    if debug and snap_val:
        wait("V", sq_o, snap_val)

    def emit_qkv_unit(w, u, first_gate):
        buf = w % 2
        b = 4 + (u % 2)
        wait("T", s_v, p45_free_v[b])
        if u < 16:
            wsb, dst = (wq_s, q_w2[buf]) if u < 8 else (wk_s, k_w2[buf])
            m = u % 8
            if w == 0 and m == 0:
                wait("T", s_v, attw_done[0 if u < 8 else 1])
            mm_done = 0
            for k in range(NCC):
                fn = (lambda t, wsb=wsb, m=m, k=k, b=b, w=w:
                      t.matmul(pb[b][:, 0:WIN], wsb[:, k, m * 128:(m + 1) * 128],
                               xn_f[:, k, w * WIN:(w + 1) * WIN],
                               start=(k == 0), stop=(k == NCC - 1)))
                if k == NCC - 1:
                    mm_done = inc("T", fn, s_t)
                else:
                    emit("T", fn)
            wait("V", s_t, max(mm_done, first_gate))
            p45_free_v[b] = inc("V", lambda v, dst=dst, m=m, b=b:
                                v.tensor_copy(dst[:, m, :], pb[b][:, 0:WIN]), s_v)
        else:
            tc2, os = divmod(u - 16, 2)
            if w == 0 and u == 16:
                wait("T", s_v, attw_done[2])
            mm_done = 0
            for k in range(NCC):
                fn = (lambda t, tc2=tc2, os=os, k=k, b=b, w=w:
                      t.matmul(pb[b][:, :], xn_f[:, k, (2 * w + tc2) * 128:(2 * w + tc2 + 1) * 128],
                               wv_s[:, k, os * 512:(os + 1) * 512],
                               start=(k == 0), stop=(k == NCC - 1)))
                if k == NCC - 1:
                    mm_done = inc("T", fn, s_t)
                else:
                    emit("T", fn)
            wait("V", s_t, max(mm_done, first_gate))
            p45_free_v[b] = inc("V", lambda v, buf=buf, tc2=tc2, os=os, b=b:
                                v.tensor_copy(v_w2[buf][:, tc2, os * 512:(os + 1) * 512],
                                              pb[b][:, :]), s_v)

    w0_vw_jobs = attw_jobs(2, wv_s, f1_last[NG - 1])
    w0_wp_jobs = attw_jobs(3, wp_s, f2_last[NG - 1])
    for u in range(20):
        if u < 16 and w0_vw_jobs:
            drain_piece(w0_vw_jobs.pop(0), (0, 1))
        emit_qkv_unit(0, u, 0)
    assert not w0_vw_jobs

    for w in range(NW):
        q_w, k_w, v_w = q_w2[w % 2], k_w2[w % 2], v_w2[w % 2]
        nxt_units = list(range(20)) if w + 1 < NW else []
        pend_yevac = {}

        def flush_yevac(hh):
            if hh in pend_yevac:
                ymm0, rho0, hp0, yb0, off0, rsl0 = pend_yevac.pop(hh)
                wait("V", s_t, ymm0)
                wait("V", s_s, rho0)
                done = inc("V", lambda v, hp=hp0, yb=yb0, off=off0, rsl=rsl0:
                           v.tensor_tensor(y_w[off:off + 64, hp, :], pb[yb][off:off + 64, 0:WIN],
                                           rho_t[off:off + 64, rsl, :], OP.mult), s_v)
                yb_free_v[hh] = done
                rho_used_v[rsl0] = done

        for hp in range(NH // 2):
            par = hp % 2
            ph_exp = {}
            ph_rho = {}
            for hh in range(2):
                h = 2 * hp + hh
                sb = hh
                off = hh * 64
                wait("T", s_s, s_exp_done[sb])
                smm = 0
                for jc in range(2):
                    fn = (lambda t, h=h, jc=jc, sb=sb, off=off, k_w=k_w, q_w=q_w:
                          t.matmul(pb[sb][:, jc * WIN:(jc + 1) * WIN],
                                   k_w[off:off + 64, h // 2, jc * 128:(jc + 1) * 128],
                                   q_w[off:off + 64, h // 2, :],
                                   start=True, stop=True, tile_position=(off, 0)))
                    if jc == 1:
                        smm = inc("T", fn, s_t)
                    else:
                        emit("T", fn)
                wait("S", s_t, smm)
                wait("S", s_t, e_used_t[hh])
                exp_done = inc_late("S", lambda s, sb=sb, hh=hh:
                                    s.activation(e_t[:, hh * 2:(hh + 1) * 2, :].rearrange("p a b -> p (a b)"),
                                                 pb[sb][:, :], AF.Exp, bias=cst[:, 0:1], scale=0.125), s_s)
                s_exp_done[sb] = exp_done
                ph_exp[hh] = exp_done
            for hh in range(2):
                lb = 6 + hh
                wait("T", s_s, ph_exp[hh])
                wait("T", s_s, l_rho_done[lb])
                lmm = 0
                for jc in range(2):
                    fn = (lambda t, jc=jc, lb=lb, hh=hh:
                          t.matmul(pb[lb][:, 0:WIN], ones[:, 0:128], e_t[:, hh * 2 + jc, :],
                                   start=(jc == 0), stop=(jc == 1)))
                    if jc == 1:
                        lmm = inc("T", fn, s_t)
                    else:
                        emit("T", fn)
                e_used_t[hh] = max(e_used_t[hh], lmm)
                wait("S", s_t, lmm)
                emit("S", lambda s, lb=lb: s.activation(nl_t[:], pb[lb][:, 0:WIN], AF.Ln, bias=0.0, scale=1.0))
                rsl = 2 * par + hh
                wait("S", s_v, rho_used_v.get(rsl, 0))
                rho_done = inc_late("S", lambda s, rsl=rsl: s.activation(rho_t[:, rsl, :], nl_t[:], AF.Exp,
                                                                          bias=0.0, scale=-1.0), s_s)
                l_rho_done[lb] = rho_done
                ph_rho[hh] = (rho_done, rsl)
            for hh in range(2):
                h = 2 * hp + hh
                off = hh * 64
                yb = 2 + hh
                flush_yevac(hh)
                wait("T", s_s, ph_exp[hh])
                wait("T", s_v, yb_free_v[hh])
                ymm = 0
                for jc in range(2):
                    fn = (lambda t, jc=jc, h=h, yb=yb, off=off, hh=hh, v_w=v_w:
                          t.matmul(pb[yb][off:off + 64, 0:WIN], v_w[:, jc, h * 64:(h + 1) * 64],
                                   e_t[:, hh * 2 + jc, :],
                                   start=(jc == 0), stop=(jc == 1),
                                   tile_position=(0, off) if off else None))
                    if jc == 1:
                        ymm = inc("T", fn, s_t)
                    else:
                        emit("T", fn)
                e_used_t[hh] = max(e_used_t[hh], ymm)
                att_used_t[par] = ymm
                pend_yevac[hh] = (ymm, ph_rho[hh][0], hp, yb, off, ph_rho[hh][1])
            for _ in range(2):
                if nxt_units:
                    emit_qkv_unit(w + 1, nxt_units.pop(0), ymm_last[w - 1])
                if w == 0 and w0_wp_jobs:
                    drain_piece(w0_wp_jobs.pop(0), (0, 1))
            if nxt_units and hp % 2 == 1:
                emit_qkv_unit(w + 1, nxt_units.pop(0), ymm_last[w - 1])
        while nxt_units:
            emit_qkv_unit(w + 1, nxt_units.pop(0), ymm_last[w - 1])
        flush_yevac(0)
        flush_yevac(1)
        y_all_evac = s_v.n
        ymm_last[w] = max(att_used_t[0], att_used_t[1])

        if w == 0:
            wait("T", s_v, attw_done[3])
        wait("T", s_v, y_all_evac)
        for tc2 in range(2):
            tc = 2 * w + tc2
            for os in range(2):
                b = 4 + ((tc2 * 2 + os) % 2)
                wait("T", s_v, p45_free_v[b])
                pmm = 0
                for k in range(NCC):
                    fn = (lambda t, k=k, tc2=tc2, os=os, b=b:
                          t.matmul(pb[b][:, :], y_w[:, k, tc2 * 128:(tc2 + 1) * 128],
                                   wp_s[:, k, os * 512:(os + 1) * 512],
                                   start=(k == 0), stop=(k == NCC - 1)))
                    if k == NCC - 1:
                        pmm = inc("T", fn, s_t)
                    else:
                        emit("T", fn)
                wait("V", s_t, pmm)
                p45_free_v[b] = inc("V", lambda v, tc=tc, os=os, b=b:
                                    v.scalar_tensor_tensor(acc[:, tc, os * 512:(os + 1) * 512],
                                                           pb[b][:, :], 1.0,
                                                           acc[:, tc, os * 512:(os + 1) * 512],
                                                           OP.mult, OP.add), s_v)
        wait("Y", s_v, s_v.n)
        for tc2 in range(2):
            tc = 2 * w + tc2
            inc("Y", lambda y, tc=tc: y.dma_start(y_h.ap()[tc * 128:(tc + 1) * 128, :], acc[:, tc, :]),
                sq_o, 16)

    wait("Y", sq_o, sq_o.n)

    if debug:
        dbg = {}
        for nm, tile in (("mean", st_mean), ("rstd", st_rstd), ("veps", st_veps),
                         ("xn_f", xn_f), ("xq_f", xq_f), ("q_w", q_w), ("k_w", k_w),
                         ("v_w", v_w), ("y_w", y_w), ("e_t", e_t), ("rho_t", rho_t),
                         ("b1q16", b1q16)):
            sh = [tile.shape[0], int(np.prod(tile.shape[1:]))]
            d_h = nc.dram_tensor("dbg_" + nm, sh, tile.dtype, kind="ExternalOutput")
            dbg[nm] = (d_h, tile)
        wait("G", s_v, s_v.n)
        wait("G", s_s, s_s.n)
        wait("G", s_t, s_t.n)
        dsem2 = SemC(es, nc, "dsem2")
        for nm, (d_h, tile) in dbg.items():
            flat = tile[:] if len(tile.shape) == 2 else tile[:].rearrange(
                {3: "p a b -> p (a b)", 4: "p a b c -> p (a b c)"}[len(tile.shape)])
            inc("G", lambda g, d_h=d_h, flat=flat: g.dma_start(d_h.ap(), flat), dsem2.h and dsem2, 16, dma=True)
        wait("G", dsem2, dsem2.n)

    with nc.Block() as block:
        @block.tensor
        def _(eng):
            for f in prog["T"]:
                f(eng)

        @block.vector
        def _(eng):
            for f in prog["V"]:
                f(eng)

        @block.scalar
        def _(eng):
            for f in prog["S"]:
                f(eng)

        @block.gpsimd
        def _(eng):
            for f in prog["G"]:
                f(eng)

        @block.sync
        def _(eng):
            for f in prog["Y"]:
                f(eng)

    es.close()
    return nc


_NC_CACHE = None


def kernel(L0_attn_x, L0_attn_w_attn, L0_attn_w_proj, L0_ffn_w1, L0_ffn_b1, L0_ffn_w2, L0_ffn_b2):
    global _NC_CACHE
    x = np.ascontiguousarray(np.asarray(L0_attn_x, dtype=np.float32).reshape(16384, 1024))
    wqkvT = np.ascontiguousarray(np.asarray(L0_attn_w_attn, dtype=np.float32).T)
    wprojT = np.ascontiguousarray(np.asarray(L0_attn_w_proj, dtype=np.float32).T)
    w1T = np.ascontiguousarray(np.asarray(L0_ffn_w1, dtype=np.float32).T)
    w2T = np.ascontiguousarray(np.asarray(L0_ffn_w2, dtype=np.float32).T)
    b1r = np.ascontiguousarray(np.asarray(L0_ffn_b1, dtype=np.float32).reshape(32, 128).T)
    b2 = np.ascontiguousarray(np.asarray(L0_ffn_b2, dtype=np.float32))

    if _NC_CACHE is None:
        _NC_CACHE = build_nc()
    nc = _NC_CACHE

    import ml_dtypes
    identf = np.eye(128, dtype=np.float32)
    onesb = np.ones((128, 128), dtype=ml_dtypes.bfloat16)
    cstv = np.zeros((128, 8), dtype=np.float32)
    cstv[:, 0] = -20.0
    cstv[:, 1] = 1e-5
    in_maps = [{
        "x": x[c * T:(c + 1) * T, :],
        "wqkvT": wqkvT, "wprojT": wprojT,
        "w1T": w1T, "w2T": w2T, "b1r": b1r, "b2": b2,
        "identf": identf, "onesb": onesb, "cstv": cstv,
    } for c in range(NCORES)]
    res = run_bass_kernel_spmd(nc, in_maps, core_ids=list(range(NCORES)))
    out = np.concatenate([np.asarray(r["y"]) for r in res.results], axis=0)
    return out.reshape(1, 16384, 1024)
